# version 04
# baseline (speedup 1.0000x reference)
import numpy as np
import ml_dtypes
import concourse.bass as bass
import concourse.mybir as mybir
from concourse.bass import IndirectOffsetOnAxis
from contextlib import ExitStack

F32 = mybir.dt.float32
F16 = mybir.dt.float16
BF16 = mybir.dt.bfloat16
I32 = mybir.dt.int32
U32 = mybir.dt.uint32
AF = mybir.ActivationFunctionType
ALU = mybir.AluOpType

L = 4096
ROWS = 1024
NT = 8
HROWS = 512
NNZ_PAD = 835840
SCAT_COLS = 204
SCAT_GCAP = 26
SCAT_BOUNDS = list(range(SCAT_GCAP, SCAT_COLS, SCAT_GCAP))
TH = 2.0 * np.pi / L


def host_csr_permute(vals, idx, ptr):
    vals2 = vals.copy()
    idx2 = idx.copy()
    cap = SCAT_GCAP
    for r in range(len(ptr) - 1):
        lo, hi = int(ptr[r]), int(ptr[r + 1])
        seg = idx[lo:hi]
        n = hi - lo
        if n <= 1 or len(np.unique(seg)) == n:
            continue
        ng = (n + cap - 1) // cap
        sizes = [min(cap, n - g * cap) for g in range(ng)]
        fill = [[] for _ in range(ng)]
        uniq, inv, cnts = np.unique(seg, return_inverse=True, return_counts=True)
        occ_of = [[] for _ in range(len(uniq))]
        for j in range(n):
            occ_of[inv[j]].append(j)
        singles = []
        for u in np.argsort(-cnts):
            if cnts[u] < 2:
                singles.extend(occ_of[u])
                continue
            gs = sorted(range(ng), key=lambda g: len(fill[g]))
            gs = [g for g in gs if len(fill[g]) < sizes[g]][:cnts[u]]
            if len(gs) < cnts[u]:
                raise RuntimeError(f"csr group packing infeasible row {r}")
            for g, j in zip(gs, occ_of[u]):
                fill[g].append(j)
        si = 0
        for g in range(ng):
            need = sizes[g] - len(fill[g])
            fill[g].extend(singles[si:si + need])
            si += need
        perm = np.asarray([j for g in range(ng) for j in fill[g]])
        vals2[lo:hi] = vals[lo + perm]
        idx2[lo:hi] = seg[perm]
    return vals2, idx2


def host_constants():
    a = np.arange(64); rho = np.arange(64); b = np.arange(64)
    sig = np.arange(64); d_ = np.arange(64); eps = np.arange(64)
    C64 = np.cos(2 * np.pi * np.outer(a, rho) / 64)
    S64 = np.sin(2 * np.pi * np.outer(a, rho) / 64)
    cs1 = np.zeros((128, 256), dtype=np.float64)
    cs1[0:64, 0:64] = C64; cs1[64:128, 64:128] = C64
    cs1[0:64, 128:192] = S64; cs1[64:128, 192:256] = S64
    lhsT2 = np.zeros((64, 128, 128), dtype=np.float64)
    for r in range(64):
        m = r + 64 * sig
        ang = TH * np.outer(b, m)
        lhsT2[r, 0:64, 0:64] = np.cos(ang)
        lhsT2[r, 64:128, 0:64] = -np.sin(ang)
        lhsT2[r, 0:64, 64:128] = np.sin(ang)
        lhsT2[r, 64:128, 64:128] = np.cos(ang)
    angi = 2 * np.pi * np.outer(sig, d_) / 64
    i1 = np.zeros((128, 128))
    i1[0:64, 0:64] = np.cos(angi); i1[64:128, 0:64] = np.cos(angi)
    i1[0:64, 64:128] = np.sin(angi); i1[64:128, 64:128] = np.sin(angi)
    i2 = np.zeros((64, 128, 64))
    for dl in range(64):
        dd = dl + 64 * eps
        ang = TH * np.outer(rho, dd)
        i2[dl, 0:64, :] = np.cos(ang) / L
        i2[dl, 64:128, :] = -np.sin(ang) / L
    bf = ml_dtypes.bfloat16
    return dict(c_s1=np.ascontiguousarray(cs1.astype(bf)),
                c_s2=np.ascontiguousarray(lhsT2.reshape(64, 128 * 128).astype(bf)),
                c_i1=np.ascontiguousarray(i1.astype(bf)),
                c_i2=np.ascontiguousarray(i2.reshape(64, 128 * 64).astype(bf)))


class Prog:
    ENGS = ("sync", "scalar", "vector", "gpsimd", "tensor")

    def __init__(self, ctx, nc):
        self.nc = nc
        self.q = {e: [] for e in self.ENGS}
        self.sem = {}
        self.cnt = {}
        for name in ("LD", "V", "A", "T", "GC",
                     "LXI", "LX2", "LV", "LBP", "LW", "LCR", "LXS",
                     "LT1a", "LT1b", "LRa", "LRb", "LBa", "LBb",
                     "LCa", "LCb", "LXa", "LXb", "LWRa", "LWRb",
                     "STa", "STb", "GL", "GCS", "GPa", "GPb", "GA0", "GA1"):
            self.sem[name] = ctx.enter_context(nc.semaphore("c" + name))
            self.cnt[name] = 0
        self.auto = {"vector": "V", "scalar": "A", "tensor": None,
                     "gpsimd": None, "sync": None}
        self.breg = None

    def n(self, name):
        return self.cnt[name]

    def op(self, eng, fn, inc=None, k=1, waits=(), auto=None):
        wl = [(c, v) for (c, v) in waits if v > 0]
        ach = self.auto[eng] if auto is None else (auto if auto else None)
        if ach is not None and inc is None:
            inc, k = ach, 1
        if ach is not None and inc == ach:
            wl.append((ach, self.cnt[ach]))
        if inc is not None:
            self.cnt[inc] += k
        semk = (self.sem[inc], k) if inc is not None else None
        sems = {c: self.sem[c] for c, _ in wl}

        def run(engq, wl=tuple(wl), semk=semk, fn=fn, sems=sems):
            for cn, v in wl:
                engq.wait_ge(sems[cn], v)
            instr = fn(engq)
            if semk is not None:
                instr.then_inc(semk[0], semk[1])
        self.q[eng].append(run)
        return self.cnt[inc] if inc is not None else None

    def wait(self, eng, *waits):
        wl = [(c, v) for (c, v) in waits if v > 0]
        sems = {c: self.sem[c] for c, _ in wl}

        def run(engq, wl=tuple(wl), sems=sems):
            for cn, v in wl:
                engq.wait_ge(sems[cn], v)
        self.q[eng].append(run)


def build(with_scatter=True):
    nc = bass.Bass("TRN2", target_bir_lowering=False)

    xin = nc.dram_tensor("xin", [ROWS, L], F32, kind="ExternalInput")
    bpin = nc.dram_tensor("bpin", [4096, 2048], I32, kind="ExternalInput")
    sc_in = nc.dram_tensor("sc_in", [128, 32], F32, kind="ExternalInput")
    ptr0_in = nc.dram_tensor("ptr0_in", [128, 32], I32, kind="ExternalInput")
    ptr1_in = nc.dram_tensor("ptr1_in", [128, 32], I32, kind="ExternalInput")
    vals_in = nc.dram_tensor("vals_in", [NNZ_PAD // 256, 256], F32, kind="ExternalInput")
    idxs_in = nc.dram_tensor("idxs_in", [NNZ_PAD // 256, 256], I32, kind="ExternalInput")
    c_s1_in = nc.dram_tensor("c_s1", [128, 256], BF16, kind="ExternalInput")
    c_s2_in = nc.dram_tensor("c_s2", [64, 128 * 128], BF16, kind="ExternalInput")
    c_i1_in = nc.dram_tensor("c_i1", [128, 128], BF16, kind="ExternalInput")
    c_i2_in = nc.dram_tensor("c_i2", [64, 128 * 64], BF16, kind="ExternalInput")
    out_hbm = nc.dram_tensor("out", [ROWS, L], F32, kind="ExternalOutput")

    wdram = nc.dram_tensor("wdram", [4096, 4096], BF16)
    vvdram = nc.dram_tensor("vvdram", [2 * ROWS, L], F16)
    t1dram = nc.dram_tensor("t1dram", [128, 64 * ROWS], BF16)
    bindram = nc.dram_tensor("bindram", [128, 64 * ROWS], BF16)
    corrT = nc.dram_tensor("corrT", [L, ROWS], BF16)
    x2dram = nc.dram_tensor("x2dram", [ROWS, L], F32)
    valm_dram = nc.dram_tensor("valm_dram", [4096, 256], F32)
    fidx_dram = nc.dram_tensor("fidx_dram", [4096, 256], U32)

    with ExitStack() as octx:
        E = octx.enter_context
        c_s1 = E(nc.sbuf_tensor("c_s1s", [128, 256], BF16))
        c_i1 = E(nc.sbuf_tensor("c_i1s", [128, 128], BF16))
        cs2b = [E(nc.sbuf_tensor(f"cs2b{i}", [128, 128], BF16)) for i in range(2)]
        ci2b = [E(nc.sbuf_tensor(f"ci2b{i}", [128, 64], BF16)) for i in range(2)]
        ident = E(nc.sbuf_tensor("ident", [128, 128], BF16))
        identf = E(nc.sbuf_tensor("identf", [128, 128], F32))
        iota256 = E(nc.sbuf_tensor("iota256", [128, 256], F32))
        rowb = E(nc.sbuf_tensor("rowb", [128, 1], F32))
        rowg = E(nc.sbuf_tensor("rowg", [128, 1], F32))
        scs = E(nc.sbuf_tensor("scs", [128, 32], F32))
        b8s = E(nc.sbuf_tensor("b8s", [128, 32], F32))
        ptr0 = E(nc.sbuf_tensor("ptr0", [128, 32], I32))
        ptr1 = E(nc.sbuf_tensor("ptr1", [128, 32], I32))
        cnt_f = E(nc.sbuf_tensor("cnt_f", [128, 32], F32))
        cnt_i = E(nc.sbuf_tensor("cnt_i", [128, 32], I32))
        eps_t = E(nc.sbuf_tensor("eps_t", [128, 1], F32))
        xt = [E(nc.sbuf_tensor(f"xt{c}", [128, ROWS], BF16)) for c in range(32)]

        valg = E(nc.sbuf_tensor("valg", [128, 256], F32))
        idxg = E(nc.sbuf_tensor("idxg", [128, 256], I32))
        mask_t = E(nc.sbuf_tensor("mask_t", [128, 256], F32))
        valm = [E(nc.sbuf_tensor(f"valm{i}", [128, 256], F32)) for i in range(2)]
        fidx = [E(nc.sbuf_tensor(f"fidx{i}", [128, 256], U32)) for i in range(2)]
        fidx_i = E(nc.sbuf_tensor("fidx_i", [128, 256], I32))
        fidx_f = E(nc.sbuf_tensor("fidx_f", [128, 256], F32))
        tmp_i = E(nc.sbuf_tensor("tmp_i", [128, 256], I32))
        wctx = ExitStack()
        bp_t = wctx.enter_context(nc.sbuf_tensor("bp_t", [128, 2048], I32))
        qi_t = wctx.enter_context(nc.sbuf_tensor("qi_t", [128, 2048], I32))
        w_t = wctx.enter_context(nc.sbuf_tensor("w_t", [128, 4096], BF16))

        P = Prog(octx, nc)
        op, wait = P.op, P.wait

        for dst, src in ((c_s1, c_s1_in), (c_i1, c_i1_in), (scs, sc_in),
                         (ptr0, ptr0_in), (ptr1, ptr1_in)):
            op("sync", lambda e, d=dst, s=src: e.dma_start(out=d[:], in_=s[:]),
               inc="LD", k=16)
        ld0 = P.n("LD")
        op("gpsimd", lambda e: e.iota(identf[:], [[1, 128]], channel_multiplier=-1,
                                      allow_small_or_imprecise_dtypes=True), inc="GC")
        op("gpsimd", lambda e: e.iota(iota256[:], [[1, 256]], channel_multiplier=0,
                                      allow_small_or_imprecise_dtypes=True), inc="GC")
        op("gpsimd", lambda e: e.iota(rowb[:], [[0, 1]], channel_multiplier=4096,
                                      allow_small_or_imprecise_dtypes=True), inc="GC")
        op("gpsimd", lambda e: e.iota(rowg[:], [[0, 1]], channel_multiplier=8192,
                                      allow_small_or_imprecise_dtypes=True), inc="GC")
        gp0 = P.n("GC")
        op("vector", lambda e: e.tensor_scalar(out=ident[:], in0=identf[:], scalar1=0.0,
                                               scalar2=None, op0=ALU.is_equal),
           waits=[("GC", gp0)])
        op("vector", lambda e: e.memset(eps_t[:], 1e-5))
        nc.const_aps.register(F32, 1e-5, eps_t[:, 0:1])
        op("vector", lambda e: e.tensor_scalar(out=b8s[:], in0=scs[:], scalar1=-8.0,
                                               scalar2=None, op0=ALU.mult),
           waits=[("LD", ld0)])
        op("vector", lambda e: e.tensor_tensor(out=cnt_i[:], in0=ptr1[:], in1=ptr0[:],
                                               op=ALU.subtract))
        op("vector", lambda e: e.tensor_copy(out=cnt_f[:], in_=cnt_i[:]))
        v_p0 = P.n("V")

        prep_marks = {}
        wflat = wdram[:].rearrange("a b -> (a b)").rearrange("(a b) -> a b", b=4096)
        for T in range(32):
            pb = T % 2
            ld_bp = op("scalar", lambda e, T=T: e.dma_start(
                out=bp_t[:], in_=bpin[T * 128:(T + 1) * 128, :]), inc="LBP", k=16,
                auto=False, waits=[("V", 0 if T == 0 else P.n("V"))])
            op("gpsimd", lambda e, T=T: e.indirect_dma_start(
                out=valg[:], out_offset=None, in_=vals_in[:],
                in_offset=IndirectOffsetOnAxis(ap=ptr0[:, T:T + 1], axis=1)),
                inc="GCS", k=16,
                waits=[("LD", ld0), ("V", 0 if T == 0 else P.n("V"))])
            g_i = op("gpsimd", lambda e, T=T: e.indirect_dma_start(
                out=idxg[:], out_offset=None, in_=idxs_in[:],
                in_offset=IndirectOffsetOnAxis(ap=ptr0[:, T:T + 1], axis=1)),
                inc="GCS", k=16)
            op("vector", lambda e: e.tensor_scalar(out=qi_t[:], in0=bp_t[:],
               scalar1=15, scalar2=None, op0=ALU.bitwise_and),
               waits=[("LBP", ld_bp), ("V", v_p0), ("LW", P.n("LW"))])
            op("vector", lambda e, T=T: e.tensor_scalar(
                out=w_t[:, 0:2048], in0=qi_t[:], scalar1=scs[:, T:T + 1],
                scalar2=b8s[:, T:T + 1], op0=ALU.mult, op1=ALU.add))
            op("vector", lambda e: e.tensor_scalar(out=qi_t[:], in0=bp_t[:],
               scalar1=4, scalar2=None, op0=ALU.logical_shift_right))
            vw = op("vector", lambda e, T=T: e.tensor_scalar(
                out=w_t[:, 2048:4096], in0=qi_t[:], scalar1=scs[:, T:T + 1],
                scalar2=b8s[:, T:T + 1], op0=ALU.mult, op1=ALU.add))
            op("sync", lambda e, T=T: e.dma_start(
                out=wdram[T * 128:(T + 1) * 128, :], in_=w_t[:]),
                inc="LW", k=16, waits=[("V", vw)])
            if with_scatter:
                op("vector", lambda e, T=T: e.tensor_scalar(
                    out=mask_t[:], in0=iota256[:], scalar1=cnt_f[:, T:T + 1],
                    scalar2=None, op0=ALU.is_lt),
                    waits=[("GCS", g_i)])
                op("vector", lambda e, pb=pb: e.tensor_tensor(
                    out=valm[pb][:], in0=valg[:], in1=mask_t[:], op=ALU.mult))
                op("vector", lambda e: e.tensor_scalar(
                    out=tmp_i[:], in0=idxg[:], scalar1=1, scalar2=None,
                    op0=ALU.bitwise_and), waits=[("GCS", g_i)])
                op("vector", lambda e: e.tensor_scalar(
                    out=tmp_i[:], in0=tmp_i[:], scalar1=11, scalar2=None,
                    op0=ALU.logical_shift_left))
                op("vector", lambda e: e.tensor_scalar(
                    out=fidx_i[:], in0=idxg[:], scalar1=1, scalar2=None,
                    op0=ALU.logical_shift_right))
                op("vector", lambda e: e.tensor_tensor(
                    out=fidx_i[:], in0=fidx_i[:], in1=tmp_i[:], op=ALU.add))
                op("vector", lambda e: e.tensor_copy(out=fidx_f[:], in_=fidx_i[:]))
                op("vector", lambda e, T=T: e.tensor_scalar(
                    out=fidx_f[:], in0=fidx_f[:], scalar1=rowb[:, 0:1],
                    scalar2=float(T * 128 * 4096 - 16777216), op0=ALU.add,
                    op1=ALU.add))
                op("vector", lambda e: e.tensor_tensor(
                    out=fidx_f[:], in0=fidx_f[:], in1=mask_t[:], op=ALU.mult))
                vfi = op("vector", lambda e, pb=pb: e.tensor_scalar(
                    out=fidx[pb][:], in0=fidx_f[:], scalar1=16777216.0,
                    scalar2=None, op0=ALU.add))
                op("sync", lambda e, T=T, pb=pb: e.dma_start(
                    out=valm_dram[T * 128:(T + 1) * 128, :], in_=valm[pb][:]),
                    inc="LW", k=16, waits=[("V", vfi)])
                op("sync", lambda e, T=T, pb=pb: e.dma_start(
                    out=fidx_dram[T * 128:(T + 1) * 128, :], in_=fidx[pb][:]),
                    inc="LW", k=16)
                prep_marks[T] = P.n("LW")
        lw_total = P.n("LW")
        lbp_total = P.n("LBP")

        scat_mark_d = {}
        wv_cap = [0]
        GPCH = {0: "GPa", 1: "GPb"}

        def emit_scatter_chunk(t_lo, t_hi):
            for T in range(t_lo, t_hi):
                pb = T % 2
                gch = GPCH[pb]
                op("gpsimd", lambda e, T=T, pb=pb: e.dma_start(
                    out=valm[pb][:], in_=valm_dram[T * 128:(T + 1) * 128, :]),
                    inc="GL", k=16,
                    waits=[(gch, scat_mark_d.get(T - 2, 0)),
                           ("LW", prep_marks[31]), ("V", wv_cap[0])])
                op("gpsimd", lambda e, T=T, pb=pb: e.dma_start(
                    out=fidx[pb][:], in_=fidx_dram[T * 128:(T + 1) * 128, :]),
                    inc="GL", k=16)
                wait("gpsimd", ("GL", P.n("GL")))
                for c in range(SCAT_COLS):
                    if c in SCAT_BOUNDS:
                        wait("gpsimd", (gch, P.n(gch)))
                    op("gpsimd", lambda e, c=c, pb=pb: e.indirect_dma_start(
                        out=wflat,
                        out_offset=IndirectOffsetOnAxis(ap=fidx[pb][:, c:c + 1], axis=1),
                        in_=valm[pb][:, c:c + 1], in_offset=None,
                        compute_op=ALU.add, oob_is_err=False, bounds_check=P.breg),
                        inc=gch, k=16)
                scat_mark_d[T] = P.n(gch)

        wv_cap[0] = P.n("V")
        emit_scatter_chunk(0, 7)
        wctx.close()

        with ExitStack() as actx:
            A = actx.enter_context
            x_t = [A(nc.sbuf_tensor(f"x{i}", [128, L], F32)) for i in range(2)]
            x_t2 = x_t
            vperm = A(nc.sbuf_tensor("vperm", [128, L], BF16))
            stat = A(nc.sbuf_tensor("stat", [128, 48], F32))
            mv = A(nc.sbuf_tensor("mv", [128, 2], F32))
            rstd_t = A(nc.sbuf_tensor("rstd_t", [128, 1], F32))
            rstd = A(nc.sbuf_tensor("rstd", [128, 1], F32))
            nmr = A(nc.sbuf_tensor("nmr", [128, 1], F32))
            vbf = A(nc.sbuf_tensor("vbf", [128, L], BF16))
            vf16 = A(nc.sbuf_tensor("vf16", [128, L], F16))
            vr = A(nc.sbuf_tensor("vr", [128, 512], BF16))
            t1sb = [A(nc.sbuf_tensor(f"t1sb{i}", [128, 512], BF16)) for i in range(2)]
            rho_rhs = [A(nc.sbuf_tensor(f"rr{i}", [128, HROWS], BF16)) for i in range(2)]
            xsq = [A(nc.sbuf_tensor(f"xsq{i}", [128, HROWS], BF16)) for i in range(2)]
            binv_sb = [A(nc.sbuf_tensor(f"bs{i}", [128, HROWS], BF16)) for i in range(2)]
            del_rhs = [A(nc.sbuf_tensor(f"dr{i}", [128, HROWS], BF16)) for i in range(2)]
            cor_sb = [A(nc.sbuf_tensor(f"cs{i}", [64, HROWS], BF16)) for i in range(2)]
            corr_t = A(nc.sbuf_tensor("corr_t", [128, L], BF16))
            top8 = A(nc.sbuf_tensor("top8", [128, 8], BF16))
            t8f = A(nc.sbuf_tensor("t8f", [128, 8], F32))
            t8d = A(nc.sbuf_tensor("t8d", [128, 8], F32))
            t8e = A(nc.sbuf_tensor("t8e", [128, 8], F32))
            wsum = A(nc.sbuf_tensor("wsum", [128, 1], F32))
            wrec = A(nc.sbuf_tensor("wrec", [128, 1], F32))
            w8s = A(nc.sbuf_tensor("w8s", [128, 64], F32))
            g8s = A(nc.sbuf_tensor("g8s", [128, 64], U32))
            d8f = A(nc.sbuf_tensor("d8f", [128, 8], F32))
            d8 = A(nc.sbuf_tensor("d8", [128, 8], U32))
            g8f = A(nc.sbuf_tensor("g8f", [128, 8], F32))
            gat = [A(nc.sbuf_tensor(f"gat{i}", [128, L], F16)) for i in range(2)]
            acc16 = vf16
            y2bf = vbf
            p_tr = [A(nc.psum_tensor(f"p_tr{i}", [128, 128], BF16)) for i in range(2)]
            p_s1 = A(nc.psum_tensor("p_s1", [128, 512], F32))
            p_s2 = A(nc.psum_tensor("p_s2", [128, HROWS], F32))
            p_i1 = A(nc.psum_tensor("p_i1", [128, HROWS], F32))
            p_i2 = A(nc.psum_tensor("p_i2", [64, HROWS], F32))

            t1w = t1dram[:].rearrange("p (bp h r) -> p bp h r", h=2, r=ROWS)
            t1r = t1dram[:].rearrange("p (b r) -> p b r", r=ROWS)
            corr_w = corrT[:].rearrange("(e d) r -> d e r", d=64)
            vv_flat = vvdram[:]
            vv_w = vvdram[:].rearrange("(r two) d -> r two d", two=2)

            vv_marks = [0] * NT
            ln_marks = [0] * NT
            x2st_marks = [0] * NT
            last_s1dma = {0: 0, 1: 0}
            bm_last = {0: 0, 1: 0}
            cm_last = {0: 0, 1: 0}
            last_tr_mark = [0]
            last_sqrt = [0]
            last_ex = [0]
            last_y2tr_mark = [0]

            for H in range(2):
                for t in range(4):
                    R = H * 4 + t
                    pb = t % 2
                    xw = [("V", P.n("V"))]
                    if R < 2:
                        xw += [("LW", lw_total), ("LBP", lbp_total)]
                    xm = op("sync", lambda e, R=R, pb=pb: e.dma_start(
                        out=x_t[pb][:], in_=xin[R * 128:(R + 1) * 128, :]),
                        inc="LXI", k=16, waits=xw)
                    for ch in range(8):
                        op("vector", lambda e, ch=ch, pb=pb: e.bn_stats(
                            out=stat[:, 6 * ch:6 * ch + 6],
                            in_=x_t[pb][:, 512 * ch:512 * (ch + 1)]),
                            waits=[("LXI", xm), ("A", last_sqrt[0])] if ch == 0 else ())
                    op("vector", lambda e: e.bn_aggr(out=mv[:], in_=stat[:]))
                    va = P.n("V")
                    op("scalar", lambda e: e.activation(out=rstd_t[:], in_=mv[:, 1:2],
                       func=AF.Sqrt, bias=1e-5), waits=[("V", va)])
                    aa = P.n("A")
                    last_sqrt[0] = aa
                    op("vector", lambda e: e.reciprocal(out=rstd[:], in_=rstd_t[:]),
                       waits=[("A", aa)])
                    op("vector", lambda e: e.tensor_scalar(
                        out=nmr[:], in0=mv[:, 0:1], scalar1=rstd[:, 0:1],
                        scalar2=-1.0, op0=ALU.mult, op1=ALU.mult))
                    op("vector", lambda e, pb=pb: e.tensor_scalar(
                        out=vbf[:], in0=x_t[pb][:], scalar1=rstd[:, 0:1],
                        scalar2=nmr[:, 0:1], op0=ALU.mult, op1=ALU.add),
                        waits=[("T", max(last_tr_mark[0], last_y2tr_mark[0]))])
                    vv16 = op("vector", lambda e, pb=pb: e.tensor_scalar(
                        out=vf16[:], in0=x_t[pb][:], scalar1=rstd[:, 0:1],
                        scalar2=nmr[:, 0:1], op0=ALU.mult, op1=ALU.add),
                        waits=[("LV", vv_marks[R - 1] if R > 0 else 0)])
                    op("scalar", lambda e, R=R: e.dma_start(
                        out=vv_w[R * 128:(R + 1) * 128, 0, :], in_=vf16[:]),
                        inc="LV", k=16, auto=False, waits=[("V", vv16)])
                    vm = op("scalar", lambda e, R=R: e.dma_start(
                        out=vv_w[R * 128:(R + 1) * 128, 1, :], in_=vf16[:]),
                        inc="LV", k=16, auto=False)
                    vv_marks[R] = vm
                    op("vector", lambda e: e.tensor_copy(
                        out=vperm[:],
                        in_=vbf[:].rearrange("p (a bp j) -> p bp j a", j=2, a=64)))
                    for g in range(8):
                        for j in range(4):
                            bp2 = 4 * g + j
                            tw = op("tensor", lambda e, bp2=bp2, j=j: e.transpose(
                                p_tr[j % 2][:], vperm[:, bp2 * 128:(bp2 + 1) * 128],
                                ident[:]), inc="T", waits=[("V", P.n("V"))])
                            op("vector", lambda e, j=j: e.tensor_copy(
                                out=vr[:, j * 128:(j + 1) * 128], in_=p_tr[j % 2][:]),
                                waits=[("T", tw)])
                        for cs in range(2):
                            mm = op("tensor", lambda e, cs=cs: e.matmul(
                                p_s1[:], c_s1[:, cs * 128:(cs + 1) * 128], vr[:],
                                start=True, stop=True), inc="T",
                                waits=[("V", P.n("V")), ("LD", ld0),
                                       ("A", P.n("A"))])
                            pb2 = (2 * g + cs) % 2
                            tch = "LT1a" if pb2 == 0 else "LT1b"
                            ev = op("scalar", lambda e, pb2=pb2: e.activation(
                                out=t1sb[pb2][:], in_=p_s1[:], func=AF.Copy),
                                waits=[("T", mm), (tch, last_s1dma[pb2])])
                            op("sync", lambda e, R=R, g=g, cs=cs, pb2=pb2:
                                e.dma_start(
                                    out=t1w[cs * 64:cs * 64 + 64, 4 * g:4 * g + 4,
                                            0, R * 128:(R + 1) * 128],
                                    in_=t1sb[pb2][0:64, :].rearrange(
                                        "p (b r) -> p b r", b=4)),
                                inc=tch, k=16, waits=[("A", ev)])
                            dm = op("sync", lambda e, R=R, g=g, cs=cs, pb2=pb2:
                                    e.dma_start(
                                        out=t1w[cs * 64:cs * 64 + 64, 4 * g:4 * g + 4,
                                                1, R * 128:(R + 1) * 128],
                                        in_=t1sb[pb2][64:128, :].rearrange(
                                            "p (b r) -> p b r", b=4)),
                                    inc=tch, k=16)
                            last_s1dma[pb2] = dm
                    last_tr_mark[0] = P.n("T")
                s1_done = [P.n("LT1a"), P.n("LT1b")]

                t_base = P.n("T")
                mm2_marks = {-2: t_base, -1: t_base}
                for rho in range(64):
                    pb = rho % 2
                    rch = "LRa" if pb == 0 else "LRb"
                    bch = "LBa" if pb == 0 else "LBb"
                    op("sync", lambda e, rho=rho, pb=pb: e.dma_start(
                        out=cs2b[pb][:],
                        in_=c_s2_in[rho, :].rearrange("(k m) -> k m", k=128)),
                        inc=rch, k=16,
                        waits=[("T", mm2_marks.get(rho - 2, 0))])
                    op("sync", lambda e, rho=rho, pb=pb, H=H: e.dma_start(
                        out=rho_rhs[pb][0:64, :],
                        in_=t1r[rho, :, H * 512:H * 512 + 512]),
                        inc=rch, k=16,
                        waits=[("LT1a", s1_done[0]), ("LT1b", s1_done[1]),
                               ("T", mm2_marks.get(rho - 2, 0))])
                    rl = op("sync", lambda e, rho=rho, pb=pb, H=H: e.dma_start(
                        out=rho_rhs[pb][64:128, :],
                        in_=t1r[64 + rho, :, H * 512:H * 512 + 512]),
                        inc=rch, k=16)
                    mm2 = op("tensor", lambda e, pb=pb: e.matmul(
                        p_s2[:], cs2b[pb][:], rho_rhs[pb][:], start=True, stop=True),
                        inc="T", waits=[(rch, rl), ("A", P.n("A"))])
                    sq = op("scalar", lambda e, pb=pb: e.activation(
                        out=xsq[pb][:], in_=p_s2[:], func=AF.Square),
                        waits=[("T", mm2)])
                    mi1 = op("tensor", lambda e, pb=pb: e.matmul(
                        p_i1[:], c_i1[:], xsq[pb][:], start=True, stop=True),
                        inc="T", waits=[("A", sq)])
                    mm2_marks[rho] = mi1
                    bv = op("scalar", lambda e, pb=pb: e.activation(
                        out=binv_sb[pb][:], in_=p_i1[:], func=AF.Copy),
                        waits=[("T", mi1), (bch, bm_last[pb])])
                    bm = op("sync", lambda e, rho=rho, pb=pb, H=H: e.dma_start(
                        out=bindram[:, rho * ROWS + H * 512:rho * ROWS + H * 512 + 512],
                        in_=binv_sb[pb][:]), inc=bch, k=16, waits=[("A", bv)])
                    bm_last[pb] = bm
                bin_done = [P.n("LBa"), P.n("LBb")]

                t_base2 = P.n("T")
                mi2_marks = {-2: t_base2, -1: t_base2}
                for dl in range(64):
                    pb = dl % 2
                    rch = "LRa" if pb == 0 else "LRb"
                    cch = "LCa" if pb == 0 else "LCb"
                    op("sync", lambda e, dl=dl, pb=pb: e.dma_start(
                        out=ci2b[pb][:],
                        in_=c_i2_in[dl, :].rearrange("(k m) -> k m", k=128)),
                        inc=rch, k=16,
                        waits=[("T", mi2_marks.get(dl - 2, 0))])
                    op("sync", lambda e, dl=dl, pb=pb, H=H: e.dma_start(
                        out=del_rhs[pb][0:64, :],
                        in_=bindram[dl, :].rearrange("(q r) -> q r", r=ROWS)[
                            :, H * 512:H * 512 + 512]),
                        inc=rch, k=16,
                        waits=[("LBa", bin_done[0]), ("LBb", bin_done[1]),
                               ("T", mi2_marks.get(dl - 2, 0))])
                    dle = op("sync", lambda e, dl=dl, pb=pb, H=H: e.dma_start(
                        out=del_rhs[pb][64:128, :],
                        in_=bindram[64 + dl, :].rearrange("(q r) -> q r", r=ROWS)[
                            :, H * 512:H * 512 + 512]),
                        inc=rch, k=16)
                    mi2 = op("tensor", lambda e, pb=pb: e.matmul(
                        p_i2[:], ci2b[pb][:], del_rhs[pb][:], start=True, stop=True),
                        inc="T", waits=[(rch, dle), ("A", P.n("A"))])
                    mi2_marks[dl] = mi2
                    cv = op("scalar", lambda e, pb=pb: e.activation(
                        out=cor_sb[pb][:], in_=p_i2[:], func=AF.Copy),
                        waits=[("T", mi2), (cch, cm_last[pb])])
                    cm = op("sync", lambda e, dl=dl, pb=pb, H=H: e.dma_start(
                        out=corr_w[dl, :, H * 512:H * 512 + 512],
                        in_=cor_sb[pb][:]), inc=cch, k=16, waits=[("A", cv)])
                    cm_last[pb] = cm
            cor_done = [P.n("LCa"), P.n("LCb")]
            for R in range(8):
                crm = op("sync", lambda e, R=R: e.dma_start_transpose(
                    out=corr_t[:], in_=corrT[:, R * 128:(R + 1) * 128]),
                    inc="LCR", k=16,
                    waits=[("LCa", cor_done[0]), ("LCb", cor_done[1]),
                           ("V", 0 if R == 0 else P.n("V"))])
                op("vector", lambda e: e.max(out=top8[:], in_=corr_t[:]),
                   waits=[("LCR", crm)])
                op("vector", lambda e: e.max_index(out=d8[:], in_max=top8[:],
                                                   in_values=corr_t[:]))
                op("vector", lambda e: e.tensor_copy(out=t8f[:], in_=top8[:]))
                op("vector", lambda e: e.tensor_scalar(
                    out=t8d[:], in0=t8f[:], scalar1=t8f[:, 0:1], scalar2=None,
                    op0=ALU.subtract), waits=[("A", last_ex[0])])
                vexp = P.n("V")
                ex = op("scalar", lambda e: e.activation(out=t8e[:], in_=t8d[:],
                   func=AF.Exp), waits=[("V", vexp)])
                last_ex[0] = ex
                op("vector", lambda e: e.reduce_sum(
                    out=wsum[:], in_=t8e[:], axis=mybir.AxisListType.X),
                    waits=[("A", ex)])
                op("vector", lambda e: e.reciprocal(out=wrec[:], in_=wsum[:]))
                op("vector", lambda e, R=R: e.tensor_scalar(
                    out=w8s[:, R * 8:(R + 1) * 8], in0=t8e[:], scalar1=wrec[:, 0:1],
                    scalar2=None, op0=ALU.mult))
                op("vector", lambda e: e.tensor_copy(out=d8f[:], in_=d8[:]))
                op("vector", lambda e, R=R: e.tensor_scalar(
                    out=g8f[:], in0=d8f[:], scalar1=rowg[:, 0:1],
                    scalar2=float(R * 128 * 8192), op0=ALU.add, op1=ALU.add))
                op("vector", lambda e, R=R: e.tensor_copy(
                    out=g8s[:, R * 8:(R + 1) * 8], in_=g8f[:]))
            vg8_all = P.n("V")

            _splits = {0: (7, 10), 1: (10, 13), 2: (13, 16), 3: (16, 19),
                       4: (19, 22), 5: (22, 25), 6: (25, 28), 7: (28, 32)}
            for R in range(8):
                pb = R % 2
                xch = "LXa" if pb == 0 else "LXb"
                xm2 = op("scalar", lambda e, R=R, pb=pb: e.dma_start(
                    out=x_t2[pb][:], in_=xin[R * 128:(R + 1) * 128, :]),
                    inc="LX2", k=16, auto=False,
                    waits=[("V", P.n("V")),
                           (xch, x2st_marks[R - 2] if R >= 2 else 0)])
                wait("gpsimd", ("V", vg8_all if R == 0 else P.n("V")),
                     ("LV", vv_marks[7]))
                gm = []
                mac_marks = []
                for k in range(8):
                    ach = "GA0" if k % 2 == 0 else "GA1"
                    gm.append(op("gpsimd", lambda e, R=R, k=k: e.indirect_dma_start(
                        out=gat[k % 2][:], out_offset=None, in_=vv_flat,
                        in_offset=IndirectOffsetOnAxis(
                            ap=g8s[:, R * 8 + k:R * 8 + k + 1], axis=1)),
                        inc=ach, k=16,
                        waits=[] if k < 2 else [("V", mac_marks[k - 2])]))
                    if k == 0:
                        op("vector", lambda e, R=R: e.tensor_scalar(
                            out=acc16[:], in0=gat[0][:],
                            scalar1=w8s[:, R * 8:R * 8 + 1],
                            scalar2=None, op0=ALU.mult),
                            waits=[(ach, gm[0])])
                        mac_marks.append(P.n("V"))
                    else:
                        op("vector", lambda e, R=R, k=k: e.scalar_tensor_tensor(
                            out=acc16[:], in0=gat[k % 2][:],
                            scalar=w8s[:, R * 8 + k:R * 8 + k + 1],
                            in1=acc16[:], op0=ALU.mult, op1=ALU.add),
                            waits=[(ach, gm[k])])
                        mac_marks.append(P.n("V"))
                op("vector", lambda e, pb=pb: e.tensor_tensor(
                    out=x_t2[pb][:], in0=x_t2[pb][:], in1=acc16[:], op=ALU.add),
                    waits=[("LX2", xm2)])
                x2st_marks[R] = op("sync", lambda e, R=R, pb=pb: e.dma_start(
                    out=x2dram[R * 128:(R + 1) * 128, :], in_=x_t2[pb][:]),
                    inc=xch, k=16, waits=[("V", P.n("V"))])
                for ch in range(8):
                    op("vector", lambda e, ch=ch, pb=pb: e.bn_stats(
                        out=stat[:, 6 * ch:6 * ch + 6],
                        in_=x_t2[pb][:, 512 * ch:512 * (ch + 1)]),
                        waits=[("A", last_sqrt[0])] if ch == 0 else ())
                op("vector", lambda e: e.bn_aggr(out=mv[:], in_=stat[:]))
                va2 = P.n("V")
                op("scalar", lambda e: e.activation(out=rstd_t[:], in_=mv[:, 1:2],
                   func=AF.Sqrt, bias=1e-5), waits=[("V", va2)])
                aa2 = P.n("A")
                last_sqrt[0] = aa2
                op("vector", lambda e: e.reciprocal(out=rstd[:], in_=rstd_t[:]),
                   waits=[("A", aa2)])
                op("vector", lambda e: e.tensor_scalar(
                    out=nmr[:], in0=mv[:, 0:1], scalar1=rstd[:, 0:1],
                    scalar2=-1.0, op0=ALU.mult, op1=ALU.mult))
                op("vector", lambda e, pb=pb: e.tensor_scalar(
                    out=y2bf[:], in0=x_t2[pb][:], scalar1=rstd[:, 0:1],
                    scalar2=nmr[:, 0:1], op0=ALU.mult, op1=ALU.add),
                    waits=[("T", max(last_y2tr_mark[0], last_tr_mark[0]))])
                y2r = y2bf[:].rearrange("p (q two) -> p two q", two=2)
                for c in range(32):
                    tw = op("tensor", lambda e, c=c: e.transpose(
                        p_tr[c % 2][:],
                        y2r[:, c // 16, (c % 16) * 128:(c % 16) * 128 + 128],
                        ident[:]), inc="T", waits=[("V", P.n("V"))])
                    op("vector", lambda e, c=c, R=R: e.tensor_copy(
                        out=xt[c][:, R * 128:(R + 1) * 128], in_=p_tr[c % 2][:]),
                        waits=[("T", tw)])
                last_y2tr_mark[0] = P.n("T")
                ln_marks[R] = P.n("V")
                emit_scatter_chunk(*_splits[R])

        v2b_total = P.n("V")
        a2b_total = P.n("A")
        lx2_total = P.n("LX2")
        lxa_total = P.n("LXa")
        lxb_total = P.n("LXb")
        with ExitStack() as fctx:
            Fa = fctx.enter_context
            wrd = [Fa(nc.sbuf_tensor(f"wrd{i}", [128, 512], BF16)) for i in range(2)]
            x2s = [Fa(nc.sbuf_tensor(f"x2s{i}", [128, 512], F32)) for i in range(2)]
            osb = [Fa(nc.sbuf_tensor(f"osb{i}", [128, 512], F32)) for i in range(2)]
            p_f = [Fa(nc.psum_tensor(f"pf{r}", [128, 512], F32)) for r in range(8)]

            ev_marks = [0] * 8
            st_marks = {}
            mmg_marks = {}
            for OB in range(8):
                for c in range(32):
                    gi = OB * 32 + c
                    pb = gi % 2
                    wch = "LWRa" if pb == 0 else "LWRb"
                    if c == 0:
                        wws = [("GPa", scat_mark_d[4 * OB + 2]),
                               ("GPb", scat_mark_d[4 * OB + 3]),
                               ("LW", lw_total)]
                        if OB == 0:
                            wws += [("V", v2b_total), ("LX2", lx2_total),
                                    ("LXa", lxa_total), ("LXb", lxb_total)]
                    else:
                        wws = [("T", mmg_marks.get(gi - 2, 0))]
                    wm = op("sync", lambda e, OB=OB, c=c, pb=pb: e.dma_start_transpose(
                        out=wrd[pb][:],
                        in_=wdram[OB * 512:(OB + 1) * 512, c * 128:(c + 1) * 128]),
                        inc=wch, k=16, waits=wws)
                    for r in range(8):
                        ws = [(wch, wm)] if r == 0 else []
                        if c == 0:
                            ws += [("V", ln_marks[r]), ("V", ev_marks[r]),
                                   ("A", a2b_total)]
                        op("tensor", lambda e, c=c, r=r, pb=pb: e.matmul(
                            p_f[r][:], xt[c][:, r * 128:(r + 1) * 128], wrd[pb][:],
                            start=(c == 0), stop=(c == 31)), inc="T", waits=ws)
                    mmg_marks[gi] = P.n("T")
                mmend = P.n("T")
                for r in range(8):
                    pb = r % 2
                    xch = "LXa" if pb == 0 else "LXb"
                    j = OB * 8 + r
                    sch = "STa" if j % 2 == 0 else "STb"
                    xws = [("V", v2b_total if (OB == 0 and r == 0) else P.n("V")),
                           (xch, x2st_marks[r])]
                    if OB == 0 and r == 0:
                        xws += [("LXa", lxa_total), ("LXb", lxb_total),
                                ("LX2", lx2_total)]
                    xm = op("scalar", lambda e, OB=OB, r=r, pb=pb: e.dma_start(
                        out=x2s[pb][:],
                        in_=x2dram[r * 128:(r + 1) * 128, OB * 512:(OB + 1) * 512]),
                        inc="LXS", k=16, auto=False, waits=xws)
                    op("vector", lambda e, r=r, pb=pb: e.tensor_tensor(
                        out=osb[pb][:], in0=x2s[pb][:], in1=p_f[r][:], op=ALU.add),
                        waits=[("LXS", xm), ("T", mmend),
                               (sch, st_marks.get(j - 2, 0))])
                    ev_marks[r] = P.n("V")
                    op("sync", lambda e, OB=OB, r=r, pb=pb: e.dma_start(
                        out=out_hbm[r * 128:(r + 1) * 128, OB * 512:(OB + 1) * 512],
                        in_=osb[pb][:]), inc=sch, k=16,
                        waits=[("V", ev_marks[r])])
                    st_marks[j] = P.n(sch)
            wait("sync", ("STa", P.n("STa")), ("STb", P.n("STb")))

        block = octx.enter_context(nc.Block())

        @block.sync
        def _(e):
            for f in P.q["sync"]:
                f(e)

        @block.scalar
        def _(e):
            for f in P.q["scalar"]:
                f(e)

        @block.vector
        def _(e):
            for f in P.q["vector"]:
                f(e)

        @block.gpsimd
        def _(e):
            with e.register("rb") as rb:
                e.reg_mov(rb, 4096 * 4096 - 1)
                P.breg = rb
                for f in P.q["gpsimd"]:
                    f(e)

        @block.tensor
        def _(e):
            for f in P.q["tensor"]:
                f(e)

    return nc


def host_inputs_for_core(core, X2d, bp, scales, vals, idx, ptr, consts):
    return dict(
        xin=np.ascontiguousarray(X2d[core * ROWS:(core + 1) * ROWS]),
        bpin=bp,
        sc_in=np.ascontiguousarray(scales.reshape(32, 128).T),
        ptr0_in=np.ascontiguousarray(ptr[:4096].reshape(32, 128).T),
        ptr1_in=np.ascontiguousarray(ptr[1:4097].reshape(32, 128).T),
        vals_in=np.ascontiguousarray(
            np.concatenate([vals, np.zeros(NNZ_PAD - len(vals), vals.dtype)]
                           ).reshape(-1, 256)),
        idxs_in=np.ascontiguousarray(
            np.concatenate([idx, np.zeros(NNZ_PAD - len(idx), idx.dtype)]
                           ).reshape(-1, 256)),
        **consts,
    )








from concourse.bass_utils import run_bass_kernel_spmd

_NC_CACHE = {}


def kernel(L0_attn_x, L0_ffn_base_packed, L0_ffn_scales, L0_ffn_ortho_vals,
           L0_ffn_ortho_indices, L0_ffn_ortho_ptr):
    X2d = np.ascontiguousarray(np.asarray(L0_attn_x, dtype=np.float32).reshape(-1, L))
    bp = np.ascontiguousarray(np.asarray(L0_ffn_base_packed, dtype=np.int32))
    scales = np.asarray(L0_ffn_scales, dtype=np.float32)
    vals = np.asarray(L0_ffn_ortho_vals, dtype=np.float32)
    idx = np.asarray(L0_ffn_ortho_indices, dtype=np.int32)
    ptr = np.asarray(L0_ffn_ortho_ptr, dtype=np.int32)
    consts = host_constants()
    pv, pi = host_csr_permute(vals, idx, ptr)
    if "nc" not in _NC_CACHE:
        _NC_CACHE["nc"] = build()
    nc = _NC_CACHE["nc"]
    in_maps = [host_inputs_for_core(c, X2d, bp, scales, pv, pi, ptr, consts)
               for c in range(8)]
    res = run_bass_kernel_spmd(nc, in_maps, core_ids=list(range(8)))
    out = np.concatenate([res.results[c]["out"] for c in range(8)], axis=0)
    return out.reshape(4, 2048, L).astype(np.float32)


# version 05 (final)
# speedup vs baseline: 1.0497x; 1.0497x over previous
import numpy as np
import ml_dtypes
import concourse.bass as bass
import concourse.mybir as mybir
from concourse.bass import IndirectOffsetOnAxis
from contextlib import ExitStack

F32 = mybir.dt.float32
F16 = mybir.dt.float16
BF16 = mybir.dt.bfloat16
I32 = mybir.dt.int32
U32 = mybir.dt.uint32
AF = mybir.ActivationFunctionType
ALU = mybir.AluOpType

L = 4096
ROWS = 1024
NT = 8
HROWS = 512
NNZ_PAD = 835840
SCAT_COLS = 204
SCAT_GCAP = 26
SCAT_BOUNDS = list(range(SCAT_GCAP, SCAT_COLS, SCAT_GCAP))
TH = 2.0 * np.pi / L


def host_csr_permute(vals, idx, ptr):
    vals2 = vals.copy()
    idx2 = idx.copy()
    cap = SCAT_GCAP
    for r in range(len(ptr) - 1):
        lo, hi = int(ptr[r]), int(ptr[r + 1])
        seg = idx[lo:hi]
        n = hi - lo
        if n <= 1 or len(np.unique(seg)) == n:
            continue
        ng = (n + cap - 1) // cap
        sizes = [min(cap, n - g * cap) for g in range(ng)]
        fill = [[] for _ in range(ng)]
        uniq, inv, cnts = np.unique(seg, return_inverse=True, return_counts=True)
        occ_of = [[] for _ in range(len(uniq))]
        for j in range(n):
            occ_of[inv[j]].append(j)
        singles = []
        for u in np.argsort(-cnts):
            if cnts[u] < 2:
                singles.extend(occ_of[u])
                continue
            gs = sorted(range(ng), key=lambda g: len(fill[g]))
            gs = [g for g in gs if len(fill[g]) < sizes[g]][:cnts[u]]
            if len(gs) < cnts[u]:
                raise RuntimeError(f"csr group packing infeasible row {r}")
            for g, j in zip(gs, occ_of[u]):
                fill[g].append(j)
        si = 0
        for g in range(ng):
            need = sizes[g] - len(fill[g])
            fill[g].extend(singles[si:si + need])
            si += need
        perm = np.asarray([j for g in range(ng) for j in fill[g]])
        vals2[lo:hi] = vals[lo + perm]
        idx2[lo:hi] = seg[perm]
    return vals2, idx2


def host_constants():
    a = np.arange(64); rho = np.arange(64); b = np.arange(64)
    sig = np.arange(64); d_ = np.arange(64); eps = np.arange(64)
    C64 = np.cos(2 * np.pi * np.outer(a, rho) / 64)
    S64 = np.sin(2 * np.pi * np.outer(a, rho) / 64)
    cs1 = np.zeros((128, 256), dtype=np.float64)
    cs1[0:64, 0:64] = C64; cs1[64:128, 64:128] = C64
    cs1[0:64, 128:192] = S64; cs1[64:128, 192:256] = S64
    lhsT2 = np.zeros((64, 128, 128), dtype=np.float64)
    for r in range(64):
        m = r + 64 * sig
        ang = TH * np.outer(b, m)
        lhsT2[r, 0:64, 0:64] = np.cos(ang)
        lhsT2[r, 64:128, 0:64] = -np.sin(ang)
        lhsT2[r, 0:64, 64:128] = np.sin(ang)
        lhsT2[r, 64:128, 64:128] = np.cos(ang)
    angi = 2 * np.pi * np.outer(sig, d_) / 64
    i1 = np.zeros((128, 128))
    i1[0:64, 0:64] = np.cos(angi); i1[64:128, 0:64] = np.cos(angi)
    i1[0:64, 64:128] = np.sin(angi); i1[64:128, 64:128] = np.sin(angi)
    i2 = np.zeros((64, 128, 64))
    for dl in range(64):
        dd = dl + 64 * eps
        ang = TH * np.outer(rho, dd)
        i2[dl, 0:64, :] = np.cos(ang) / L
        i2[dl, 64:128, :] = -np.sin(ang) / L
    bf = ml_dtypes.bfloat16
    return dict(c_s1=np.ascontiguousarray(cs1.astype(bf)),
                c_s2=np.ascontiguousarray(lhsT2.reshape(64, 128 * 128).astype(bf)),
                c_i1=np.ascontiguousarray(i1.astype(bf)),
                c_i2=np.ascontiguousarray(i2.reshape(64, 128 * 64).astype(bf)))


class Prog:
    ENGS = ("sync", "scalar", "vector", "gpsimd", "tensor")

    def __init__(self, ctx, nc):
        self.nc = nc
        self.q = {e: [] for e in self.ENGS}
        self.sem = {}
        self.cnt = {}
        for name in ("LD", "V", "A", "T", "GC",
                     "LXI", "LX2", "LV", "LBP", "LW", "LCR", "LXS",
                     "LT1a", "LT1b", "LRa", "LRb", "LBa", "LBb",
                     "LCa", "LCb", "LXa", "LXb", "LWRa", "LWRb",
                     "STa", "STb", "GL", "GCS", "GPa", "GPb", "GA0", "GA1"):
            self.sem[name] = ctx.enter_context(nc.semaphore("c" + name))
            self.cnt[name] = 0
        self.auto = {"vector": "V", "scalar": "A", "tensor": None,
                     "gpsimd": None, "sync": None}
        self.breg = None

    def n(self, name):
        return self.cnt[name]

    def op(self, eng, fn, inc=None, k=1, waits=(), auto=None):
        wl = [(c, v) for (c, v) in waits if v > 0]
        ach = self.auto[eng] if auto is None else (auto if auto else None)
        if ach is not None and inc is None:
            inc, k = ach, 1
        if ach is not None and inc == ach:
            wl.append((ach, self.cnt[ach]))
        if inc is not None:
            self.cnt[inc] += k
        semk = (self.sem[inc], k) if inc is not None else None
        sems = {c: self.sem[c] for c, _ in wl}

        def run(engq, wl=tuple(wl), semk=semk, fn=fn, sems=sems):
            for cn, v in wl:
                engq.wait_ge(sems[cn], v)
            instr = fn(engq)
            if semk is not None:
                instr.then_inc(semk[0], semk[1])
        self.q[eng].append(run)
        return self.cnt[inc] if inc is not None else None

    def wait(self, eng, *waits):
        wl = [(c, v) for (c, v) in waits if v > 0]
        sems = {c: self.sem[c] for c, _ in wl}

        def run(engq, wl=tuple(wl), sems=sems):
            for cn, v in wl:
                engq.wait_ge(sems[cn], v)
        self.q[eng].append(run)


def build(with_scatter=True):
    nc = bass.Bass("TRN2", target_bir_lowering=False)

    xin = nc.dram_tensor("xin", [ROWS, L], F32, kind="ExternalInput")
    bpin = nc.dram_tensor("bpin", [4096, 2048], I32, kind="ExternalInput")
    sc_in = nc.dram_tensor("sc_in", [128, 32], F32, kind="ExternalInput")
    ptr0_in = nc.dram_tensor("ptr0_in", [128, 32], I32, kind="ExternalInput")
    ptr1_in = nc.dram_tensor("ptr1_in", [128, 32], I32, kind="ExternalInput")
    vals_in = nc.dram_tensor("vals_in", [NNZ_PAD // 256, 256], F32, kind="ExternalInput")
    idxs_in = nc.dram_tensor("idxs_in", [NNZ_PAD // 256, 256], I32, kind="ExternalInput")
    c_s1_in = nc.dram_tensor("c_s1", [128, 256], BF16, kind="ExternalInput")
    c_s2_in = nc.dram_tensor("c_s2", [64, 128 * 128], BF16, kind="ExternalInput")
    c_i1_in = nc.dram_tensor("c_i1", [128, 128], BF16, kind="ExternalInput")
    c_i2_in = nc.dram_tensor("c_i2", [64, 128 * 64], BF16, kind="ExternalInput")
    out_hbm = nc.dram_tensor("out", [ROWS, L], F32, kind="ExternalOutput")

    wdram = nc.dram_tensor("wdram", [4096, 4096], BF16)
    vvdram = nc.dram_tensor("vvdram", [2 * ROWS, L], F16)
    t1dram = nc.dram_tensor("t1dram", [128, 64 * ROWS], BF16)
    bindram = nc.dram_tensor("bindram", [128, 64 * ROWS], BF16)
    corrT = nc.dram_tensor("corrT", [L, ROWS], BF16)
    x2dram = nc.dram_tensor("x2dram", [ROWS, L], F32)
    valm_dram = nc.dram_tensor("valm_dram", [4096, 256], F32)
    fidx_dram = nc.dram_tensor("fidx_dram", [4096, 256], U32)

    with ExitStack() as octx:
        E = octx.enter_context
        c_s1 = E(nc.sbuf_tensor("c_s1s", [128, 256], BF16))
        c_i1 = E(nc.sbuf_tensor("c_i1s", [128, 128], BF16))
        cs2b = [E(nc.sbuf_tensor(f"cs2b{i}", [128, 128], BF16)) for i in range(2)]
        ci2b = [E(nc.sbuf_tensor(f"ci2b{i}", [128, 64], BF16)) for i in range(2)]
        ident = E(nc.sbuf_tensor("ident", [128, 128], BF16))
        identf = E(nc.sbuf_tensor("identf", [128, 128], F32))
        iota256 = E(nc.sbuf_tensor("iota256", [128, 256], F32))
        rowb = E(nc.sbuf_tensor("rowb", [128, 1], F32))
        rowg = E(nc.sbuf_tensor("rowg", [128, 1], F32))
        scs = E(nc.sbuf_tensor("scs", [128, 32], F32))
        b8s = E(nc.sbuf_tensor("b8s", [128, 32], F32))
        ptr0 = E(nc.sbuf_tensor("ptr0", [128, 32], I32))
        ptr1 = E(nc.sbuf_tensor("ptr1", [128, 32], I32))
        cnt_f = E(nc.sbuf_tensor("cnt_f", [128, 32], F32))
        cnt_i = E(nc.sbuf_tensor("cnt_i", [128, 32], I32))
        eps_t = E(nc.sbuf_tensor("eps_t", [128, 1], F32))
        xt = [E(nc.sbuf_tensor(f"xt{c}", [128, ROWS], BF16)) for c in range(32)]

        valg = E(nc.sbuf_tensor("valg", [128, 256], F32))
        idxg = E(nc.sbuf_tensor("idxg", [128, 256], I32))
        mask_t = E(nc.sbuf_tensor("mask_t", [128, 256], F32))
        valm = [E(nc.sbuf_tensor(f"valm{i}", [128, 256], F32)) for i in range(2)]
        fidx = [E(nc.sbuf_tensor(f"fidx{i}", [128, 256], U32)) for i in range(2)]
        fidx_i = E(nc.sbuf_tensor("fidx_i", [128, 256], I32))
        fidx_f = E(nc.sbuf_tensor("fidx_f", [128, 256], F32))
        tmp_i = E(nc.sbuf_tensor("tmp_i", [128, 256], I32))
        wctx = ExitStack()
        bp_t = wctx.enter_context(nc.sbuf_tensor("bp_t", [128, 2048], I32))
        qi_t = wctx.enter_context(nc.sbuf_tensor("qi_t", [128, 2048], I32))
        w_t = wctx.enter_context(nc.sbuf_tensor("w_t", [128, 4096], BF16))

        P = Prog(octx, nc)
        op, wait = P.op, P.wait

        for dst, src in ((c_s1, c_s1_in), (c_i1, c_i1_in), (scs, sc_in),
                         (ptr0, ptr0_in), (ptr1, ptr1_in)):
            op("sync", lambda e, d=dst, s=src: e.dma_start(out=d[:], in_=s[:]),
               inc="LD", k=16)
        ld0 = P.n("LD")
        op("gpsimd", lambda e: e.iota(identf[:], [[1, 128]], channel_multiplier=-1,
                                      allow_small_or_imprecise_dtypes=True), inc="GC")
        op("gpsimd", lambda e: e.iota(iota256[:], [[1, 256]], channel_multiplier=0,
                                      allow_small_or_imprecise_dtypes=True), inc="GC")
        op("gpsimd", lambda e: e.iota(rowb[:], [[0, 1]], channel_multiplier=4096,
                                      allow_small_or_imprecise_dtypes=True), inc="GC")
        op("gpsimd", lambda e: e.iota(rowg[:], [[0, 1]], channel_multiplier=8192,
                                      allow_small_or_imprecise_dtypes=True), inc="GC")
        gp0 = P.n("GC")
        op("vector", lambda e: e.tensor_scalar(out=ident[:], in0=identf[:], scalar1=0.0,
                                               scalar2=None, op0=ALU.is_equal),
           waits=[("GC", gp0)])
        op("vector", lambda e: e.memset(eps_t[:], 1e-5))
        nc.const_aps.register(F32, 1e-5, eps_t[:, 0:1])
        op("vector", lambda e: e.tensor_scalar(out=b8s[:], in0=scs[:], scalar1=-8.0,
                                               scalar2=None, op0=ALU.mult),
           waits=[("LD", ld0)])
        op("vector", lambda e: e.tensor_tensor(out=cnt_i[:], in0=ptr1[:], in1=ptr0[:],
                                               op=ALU.subtract))
        op("vector", lambda e: e.tensor_copy(out=cnt_f[:], in_=cnt_i[:]))
        v_p0 = P.n("V")

        prep_marks = {}
        wflat = wdram[:].rearrange("a b -> (a b)").rearrange("(a b) -> a b", b=4096)
        for T in range(32):
            pb = T % 2
            ld_bp = op("scalar", lambda e, T=T: e.dma_start(
                out=bp_t[:], in_=bpin[T * 128:(T + 1) * 128, :]), inc="LBP", k=16,
                auto=False, waits=[("V", 0 if T == 0 else P.n("V"))])
            op("gpsimd", lambda e, T=T: e.indirect_dma_start(
                out=valg[:], out_offset=None, in_=vals_in[:],
                in_offset=IndirectOffsetOnAxis(ap=ptr0[:, T:T + 1], axis=1)),
                inc="GCS", k=16,
                waits=[("LD", ld0), ("V", 0 if T == 0 else P.n("V"))])
            g_i = op("gpsimd", lambda e, T=T: e.indirect_dma_start(
                out=idxg[:], out_offset=None, in_=idxs_in[:],
                in_offset=IndirectOffsetOnAxis(ap=ptr0[:, T:T + 1], axis=1)),
                inc="GCS", k=16)
            op("vector", lambda e: e.tensor_scalar(out=qi_t[:], in0=bp_t[:],
               scalar1=15, scalar2=None, op0=ALU.bitwise_and),
               waits=[("LBP", ld_bp), ("V", v_p0), ("LW", P.n("LW"))])
            op("vector", lambda e, T=T: e.tensor_scalar(
                out=w_t[:, 0:2048], in0=qi_t[:], scalar1=scs[:, T:T + 1],
                scalar2=b8s[:, T:T + 1], op0=ALU.mult, op1=ALU.add))
            op("vector", lambda e: e.tensor_scalar(out=qi_t[:], in0=bp_t[:],
               scalar1=4, scalar2=None, op0=ALU.logical_shift_right))
            vw = op("vector", lambda e, T=T: e.tensor_scalar(
                out=w_t[:, 2048:4096], in0=qi_t[:], scalar1=scs[:, T:T + 1],
                scalar2=b8s[:, T:T + 1], op0=ALU.mult, op1=ALU.add))
            op("sync", lambda e, T=T: e.dma_start(
                out=wdram[T * 128:(T + 1) * 128, :], in_=w_t[:]),
                inc="LW", k=16, waits=[("V", vw)])
            if with_scatter:
                op("vector", lambda e, T=T: e.tensor_scalar(
                    out=mask_t[:], in0=iota256[:], scalar1=cnt_f[:, T:T + 1],
                    scalar2=None, op0=ALU.is_lt),
                    waits=[("GCS", g_i)])
                op("vector", lambda e, pb=pb: e.tensor_tensor(
                    out=valm[pb][:], in0=valg[:], in1=mask_t[:], op=ALU.mult))
                op("vector", lambda e: e.tensor_scalar(
                    out=tmp_i[:], in0=idxg[:], scalar1=1, scalar2=None,
                    op0=ALU.bitwise_and), waits=[("GCS", g_i)])
                op("vector", lambda e: e.tensor_scalar(
                    out=tmp_i[:], in0=tmp_i[:], scalar1=11, scalar2=None,
                    op0=ALU.logical_shift_left))
                op("vector", lambda e: e.tensor_scalar(
                    out=fidx_i[:], in0=idxg[:], scalar1=1, scalar2=None,
                    op0=ALU.logical_shift_right))
                op("vector", lambda e: e.tensor_tensor(
                    out=fidx_i[:], in0=fidx_i[:], in1=tmp_i[:], op=ALU.add))
                op("vector", lambda e: e.tensor_copy(out=fidx_f[:], in_=fidx_i[:]))
                op("vector", lambda e, T=T: e.tensor_scalar(
                    out=fidx_f[:], in0=fidx_f[:], scalar1=rowb[:, 0:1],
                    scalar2=float(T * 128 * 4096 - 16777216), op0=ALU.add,
                    op1=ALU.add))
                op("vector", lambda e: e.tensor_tensor(
                    out=fidx_f[:], in0=fidx_f[:], in1=mask_t[:], op=ALU.mult))
                vfi = op("vector", lambda e, pb=pb: e.tensor_scalar(
                    out=fidx[pb][:], in0=fidx_f[:], scalar1=16777216.0,
                    scalar2=None, op0=ALU.add))
                op("sync", lambda e, T=T, pb=pb: e.dma_start(
                    out=valm_dram[T * 128:(T + 1) * 128, :], in_=valm[pb][:]),
                    inc="LW", k=16, waits=[("V", vfi)])
                op("sync", lambda e, T=T, pb=pb: e.dma_start(
                    out=fidx_dram[T * 128:(T + 1) * 128, :], in_=fidx[pb][:]),
                    inc="LW", k=16)
                prep_marks[T] = P.n("LW")
        lw_total = P.n("LW")
        lbp_total = P.n("LBP")

        scat_mark_d = {}
        wv_cap = [0]
        GPCH = {0: "GPa", 1: "GPb"}

        def _emit_reload(T):
            pb = T % 2
            gch = GPCH[pb]
            op("gpsimd", lambda e, T=T, pb=pb: e.dma_start(
                out=valm[pb][:], in_=valm_dram[T * 128:(T + 1) * 128, :]),
                inc="GL", k=16,
                waits=[(gch, scat_mark_d.get(T - 2, 0)),
                       ("LW", prep_marks[31]), ("V", wv_cap[0])])
            op("gpsimd", lambda e, T=T, pb=pb: e.dma_start(
                out=fidx[pb][:], in_=fidx_dram[T * 128:(T + 1) * 128, :]),
                inc="GL", k=16)

        def _emit_group(T, c_lo, c_hi, drain):
            pb = T % 2
            gch = GPCH[pb]
            if drain:
                wait("gpsimd", (gch, P.n(gch)))
            for c in range(c_lo, c_hi):
                op("gpsimd", lambda e, c=c, pb=pb: e.indirect_dma_start(
                    out=wflat,
                    out_offset=IndirectOffsetOnAxis(ap=fidx[pb][:, c:c + 1], axis=1),
                    in_=valm[pb][:, c:c + 1], in_offset=None,
                    compute_op=ALU.add, oob_is_err=False, bounds_check=P.breg),
                    inc=gch, k=16)
            scat_mark_d[T] = P.n(gch)

        GEDGES = [0] + SCAT_BOUNDS + [SCAT_COLS]

        def emit_scatter_chunk(t_lo, t_hi):
            ts = list(range(t_lo, t_hi))
            k = 0
            while k < len(ts):
                if k + 1 < len(ts):
                    Ta, Tb = ts[k], ts[k + 1]
                    _emit_reload(Ta)
                    _emit_reload(Tb)
                    wait("gpsimd", ("GL", P.n("GL")))
                    for g in range(len(GEDGES) - 1):
                        _emit_group(Ta, GEDGES[g], GEDGES[g + 1], g > 0)
                        _emit_group(Tb, GEDGES[g], GEDGES[g + 1], g > 0)
                    k += 2
                else:
                    T = ts[k]
                    _emit_reload(T)
                    wait("gpsimd", ("GL", P.n("GL")))
                    for g in range(len(GEDGES) - 1):
                        _emit_group(T, GEDGES[g], GEDGES[g + 1], g > 0)
                    k += 1

        wv_cap[0] = P.n("V")
        emit_scatter_chunk(0, 6)
        wctx.close()

        with ExitStack() as actx:
            A = actx.enter_context
            x_t = [A(nc.sbuf_tensor(f"x{i}", [128, L], F32)) for i in range(2)]
            x_t2 = x_t
            vperm = A(nc.sbuf_tensor("vperm", [128, L], BF16))
            stat = A(nc.sbuf_tensor("stat", [128, 48], F32))
            mv = A(nc.sbuf_tensor("mv", [128, 2], F32))
            rstd_t = A(nc.sbuf_tensor("rstd_t", [128, 1], F32))
            rstd = A(nc.sbuf_tensor("rstd", [128, 1], F32))
            nmr = A(nc.sbuf_tensor("nmr", [128, 1], F32))
            vbf = A(nc.sbuf_tensor("vbf", [128, L], BF16))
            vf16 = A(nc.sbuf_tensor("vf16", [128, L], F16))
            vr = A(nc.sbuf_tensor("vr", [128, 512], BF16))
            t1sb = [A(nc.sbuf_tensor(f"t1sb{i}", [128, 512], BF16)) for i in range(2)]
            rho_rhs = [A(nc.sbuf_tensor(f"rr{i}", [128, HROWS], BF16)) for i in range(2)]
            xsq = [A(nc.sbuf_tensor(f"xsq{i}", [128, HROWS], BF16)) for i in range(2)]
            binv_sb = [A(nc.sbuf_tensor(f"bs{i}", [128, HROWS], BF16)) for i in range(2)]
            del_rhs = [A(nc.sbuf_tensor(f"dr{i}", [128, HROWS], BF16)) for i in range(2)]
            cor_sb = [A(nc.sbuf_tensor(f"cs{i}", [64, HROWS], BF16)) for i in range(2)]
            corr_t = A(nc.sbuf_tensor("corr_t", [128, L], BF16))
            top8 = A(nc.sbuf_tensor("top8", [128, 8], BF16))
            t8f = A(nc.sbuf_tensor("t8f", [128, 8], F32))
            t8d = A(nc.sbuf_tensor("t8d", [128, 8], F32))
            t8e = A(nc.sbuf_tensor("t8e", [128, 8], F32))
            wsum = A(nc.sbuf_tensor("wsum", [128, 1], F32))
            wrec = A(nc.sbuf_tensor("wrec", [128, 1], F32))
            w8s = A(nc.sbuf_tensor("w8s", [128, 64], F32))
            g8s = A(nc.sbuf_tensor("g8s", [128, 64], U32))
            d8f = A(nc.sbuf_tensor("d8f", [128, 8], F32))
            d8 = A(nc.sbuf_tensor("d8", [128, 8], U32))
            g8f = A(nc.sbuf_tensor("g8f", [128, 8], F32))
            gat = [A(nc.sbuf_tensor(f"gat{i}", [128, L], F16)) for i in range(2)]
            acc16 = vf16
            y2bf = vbf
            p_tr = [A(nc.psum_tensor(f"p_tr{i}", [128, 128], BF16)) for i in range(2)]
            p_s1 = A(nc.psum_tensor("p_s1", [128, 512], F32))
            p_s2 = A(nc.psum_tensor("p_s2", [128, HROWS], F32))
            p_i1 = A(nc.psum_tensor("p_i1", [128, HROWS], F32))
            p_i2 = A(nc.psum_tensor("p_i2", [64, HROWS], F32))

            t1w = t1dram[:].rearrange("p (bp h r) -> p bp h r", h=2, r=ROWS)
            t1r = t1dram[:].rearrange("p (b r) -> p b r", r=ROWS)
            corr_w = corrT[:].rearrange("(e d) r -> d e r", d=64)
            vv_flat = vvdram[:]
            vv_w = vvdram[:].rearrange("(r two) d -> r two d", two=2)

            vv_marks = [0] * NT
            ln_marks = [0] * NT
            x2st_marks = [0] * NT
            last_s1dma = {0: 0, 1: 0}
            bm_last = {0: 0, 1: 0}
            cm_last = {0: 0, 1: 0}
            last_tr_mark = [0]
            last_sqrt = [0]
            last_ex = [0]
            last_y2tr_mark = [0]

            for H in range(2):
                for t in range(4):
                    R = H * 4 + t
                    pb = t % 2
                    xw = [("V", P.n("V"))]
                    if R < 2:
                        xw += [("LW", lw_total), ("LBP", lbp_total)]
                    xm = op("sync", lambda e, R=R, pb=pb: e.dma_start(
                        out=x_t[pb][:], in_=xin[R * 128:(R + 1) * 128, :]),
                        inc="LXI", k=16, waits=xw)
                    for ch in range(8):
                        op("vector", lambda e, ch=ch, pb=pb: e.bn_stats(
                            out=stat[:, 6 * ch:6 * ch + 6],
                            in_=x_t[pb][:, 512 * ch:512 * (ch + 1)]),
                            waits=[("LXI", xm), ("A", last_sqrt[0])] if ch == 0 else ())
                    op("vector", lambda e: e.bn_aggr(out=mv[:], in_=stat[:]))
                    va = P.n("V")
                    op("scalar", lambda e: e.activation(out=rstd_t[:], in_=mv[:, 1:2],
                       func=AF.Sqrt, bias=1e-5), waits=[("V", va)])
                    aa = P.n("A")
                    last_sqrt[0] = aa
                    op("vector", lambda e: e.reciprocal(out=rstd[:], in_=rstd_t[:]),
                       waits=[("A", aa)])
                    op("vector", lambda e: e.tensor_scalar(
                        out=nmr[:], in0=mv[:, 0:1], scalar1=rstd[:, 0:1],
                        scalar2=-1.0, op0=ALU.mult, op1=ALU.mult))
                    op("vector", lambda e, pb=pb: e.tensor_scalar(
                        out=vbf[:], in0=x_t[pb][:], scalar1=rstd[:, 0:1],
                        scalar2=nmr[:, 0:1], op0=ALU.mult, op1=ALU.add),
                        waits=[("T", max(last_tr_mark[0], last_y2tr_mark[0]))])
                    vv16 = op("vector", lambda e, pb=pb: e.tensor_scalar(
                        out=vf16[:], in0=x_t[pb][:], scalar1=rstd[:, 0:1],
                        scalar2=nmr[:, 0:1], op0=ALU.mult, op1=ALU.add),
                        waits=[("LV", vv_marks[R - 1] if R > 0 else 0)])
                    op("scalar", lambda e, R=R: e.dma_start(
                        out=vv_w[R * 128:(R + 1) * 128, 0, :], in_=vf16[:]),
                        inc="LV", k=16, auto=False, waits=[("V", vv16)])
                    vm = op("scalar", lambda e, R=R: e.dma_start(
                        out=vv_w[R * 128:(R + 1) * 128, 1, :], in_=vf16[:]),
                        inc="LV", k=16, auto=False)
                    vv_marks[R] = vm
                    op("vector", lambda e: e.tensor_copy(
                        out=vperm[:],
                        in_=vbf[:].rearrange("p (a bp j) -> p bp j a", j=2, a=64)))
                    for g in range(8):
                        for j in range(4):
                            bp2 = 4 * g + j
                            tw = op("tensor", lambda e, bp2=bp2, j=j: e.transpose(
                                p_tr[j % 2][:], vperm[:, bp2 * 128:(bp2 + 1) * 128],
                                ident[:]), inc="T", waits=[("V", P.n("V"))])
                            op("vector", lambda e, j=j: e.tensor_copy(
                                out=vr[:, j * 128:(j + 1) * 128], in_=p_tr[j % 2][:]),
                                waits=[("T", tw)])
                        for cs in range(2):
                            mm = op("tensor", lambda e, cs=cs: e.matmul(
                                p_s1[:], c_s1[:, cs * 128:(cs + 1) * 128], vr[:],
                                start=True, stop=True), inc="T",
                                waits=[("V", P.n("V")), ("LD", ld0),
                                       ("A", P.n("A"))])
                            pb2 = (2 * g + cs) % 2
                            tch = "LT1a" if pb2 == 0 else "LT1b"
                            ev = op("scalar", lambda e, pb2=pb2: e.activation(
                                out=t1sb[pb2][:], in_=p_s1[:], func=AF.Copy),
                                waits=[("T", mm), (tch, last_s1dma[pb2])])
                            op("sync", lambda e, R=R, g=g, cs=cs, pb2=pb2:
                                e.dma_start(
                                    out=t1w[cs * 64:cs * 64 + 64, 4 * g:4 * g + 4,
                                            0, R * 128:(R + 1) * 128],
                                    in_=t1sb[pb2][0:64, :].rearrange(
                                        "p (b r) -> p b r", b=4)),
                                inc=tch, k=16, waits=[("A", ev)])
                            dm = op("sync", lambda e, R=R, g=g, cs=cs, pb2=pb2:
                                    e.dma_start(
                                        out=t1w[cs * 64:cs * 64 + 64, 4 * g:4 * g + 4,
                                                1, R * 128:(R + 1) * 128],
                                        in_=t1sb[pb2][64:128, :].rearrange(
                                            "p (b r) -> p b r", b=4)),
                                    inc=tch, k=16)
                            last_s1dma[pb2] = dm
                    last_tr_mark[0] = P.n("T")
                s1_done = [P.n("LT1a"), P.n("LT1b")]

                t_base = P.n("T")
                mm2_marks = {-2: t_base, -1: t_base}
                for rho in range(64):
                    pb = rho % 2
                    rch = "LRa" if pb == 0 else "LRb"
                    bch = "LBa" if pb == 0 else "LBb"
                    op("sync", lambda e, rho=rho, pb=pb: e.dma_start(
                        out=cs2b[pb][:],
                        in_=c_s2_in[rho, :].rearrange("(k m) -> k m", k=128)),
                        inc=rch, k=16,
                        waits=[("T", mm2_marks.get(rho - 2, 0))])
                    op("sync", lambda e, rho=rho, pb=pb, H=H: e.dma_start(
                        out=rho_rhs[pb][0:64, :],
                        in_=t1r[rho, :, H * 512:H * 512 + 512]),
                        inc=rch, k=16,
                        waits=[("LT1a", s1_done[0]), ("LT1b", s1_done[1]),
                               ("T", mm2_marks.get(rho - 2, 0))])
                    rl = op("sync", lambda e, rho=rho, pb=pb, H=H: e.dma_start(
                        out=rho_rhs[pb][64:128, :],
                        in_=t1r[64 + rho, :, H * 512:H * 512 + 512]),
                        inc=rch, k=16)
                    mm2 = op("tensor", lambda e, pb=pb: e.matmul(
                        p_s2[:], cs2b[pb][:], rho_rhs[pb][:], start=True, stop=True),
                        inc="T", waits=[(rch, rl), ("A", P.n("A"))])
                    sq = op("scalar", lambda e, pb=pb: e.activation(
                        out=xsq[pb][:], in_=p_s2[:], func=AF.Square),
                        waits=[("T", mm2)])
                    mi1 = op("tensor", lambda e, pb=pb: e.matmul(
                        p_i1[:], c_i1[:], xsq[pb][:], start=True, stop=True),
                        inc="T", waits=[("A", sq)])
                    mm2_marks[rho] = mi1
                    bv = op("scalar", lambda e, pb=pb: e.activation(
                        out=binv_sb[pb][:], in_=p_i1[:], func=AF.Copy),
                        waits=[("T", mi1), (bch, bm_last[pb])])
                    bm = op("sync", lambda e, rho=rho, pb=pb, H=H: e.dma_start(
                        out=bindram[:, rho * ROWS + H * 512:rho * ROWS + H * 512 + 512],
                        in_=binv_sb[pb][:]), inc=bch, k=16, waits=[("A", bv)])
                    bm_last[pb] = bm
                bin_done = [P.n("LBa"), P.n("LBb")]

                t_base2 = P.n("T")
                mi2_marks = {-2: t_base2, -1: t_base2}
                for dl in range(64):
                    pb = dl % 2
                    rch = "LRa" if pb == 0 else "LRb"
                    cch = "LCa" if pb == 0 else "LCb"
                    op("sync", lambda e, dl=dl, pb=pb: e.dma_start(
                        out=ci2b[pb][:],
                        in_=c_i2_in[dl, :].rearrange("(k m) -> k m", k=128)),
                        inc=rch, k=16,
                        waits=[("T", mi2_marks.get(dl - 2, 0))])
                    op("sync", lambda e, dl=dl, pb=pb, H=H: e.dma_start(
                        out=del_rhs[pb][0:64, :],
                        in_=bindram[dl, :].rearrange("(q r) -> q r", r=ROWS)[
                            :, H * 512:H * 512 + 512]),
                        inc=rch, k=16,
                        waits=[("LBa", bin_done[0]), ("LBb", bin_done[1]),
                               ("T", mi2_marks.get(dl - 2, 0))])
                    dle = op("sync", lambda e, dl=dl, pb=pb, H=H: e.dma_start(
                        out=del_rhs[pb][64:128, :],
                        in_=bindram[64 + dl, :].rearrange("(q r) -> q r", r=ROWS)[
                            :, H * 512:H * 512 + 512]),
                        inc=rch, k=16)
                    mi2 = op("tensor", lambda e, pb=pb: e.matmul(
                        p_i2[:], ci2b[pb][:], del_rhs[pb][:], start=True, stop=True),
                        inc="T", waits=[(rch, dle), ("A", P.n("A"))])
                    mi2_marks[dl] = mi2
                    cv = op("scalar", lambda e, pb=pb: e.activation(
                        out=cor_sb[pb][:], in_=p_i2[:], func=AF.Copy),
                        waits=[("T", mi2), (cch, cm_last[pb])])
                    cm = op("sync", lambda e, dl=dl, pb=pb, H=H: e.dma_start(
                        out=corr_w[dl, :, H * 512:H * 512 + 512],
                        in_=cor_sb[pb][:]), inc=cch, k=16, waits=[("A", cv)])
                    cm_last[pb] = cm
            cor_done = [P.n("LCa"), P.n("LCb")]
            for R in range(8):
                crm = op("sync", lambda e, R=R: e.dma_start_transpose(
                    out=corr_t[:], in_=corrT[:, R * 128:(R + 1) * 128]),
                    inc="LCR", k=16,
                    waits=[("LCa", cor_done[0]), ("LCb", cor_done[1]),
                           ("V", 0 if R == 0 else P.n("V"))])
                op("vector", lambda e: e.max(out=top8[:], in_=corr_t[:]),
                   waits=[("LCR", crm)])
                op("vector", lambda e: e.max_index(out=d8[:], in_max=top8[:],
                                                   in_values=corr_t[:]))
                op("vector", lambda e: e.tensor_copy(out=t8f[:], in_=top8[:]))
                op("vector", lambda e: e.tensor_scalar(
                    out=t8d[:], in0=t8f[:], scalar1=t8f[:, 0:1], scalar2=None,
                    op0=ALU.subtract), waits=[("A", last_ex[0])])
                vexp = P.n("V")
                ex = op("scalar", lambda e: e.activation(out=t8e[:], in_=t8d[:],
                   func=AF.Exp), waits=[("V", vexp)])
                last_ex[0] = ex
                op("vector", lambda e: e.reduce_sum(
                    out=wsum[:], in_=t8e[:], axis=mybir.AxisListType.X),
                    waits=[("A", ex)])
                op("vector", lambda e: e.reciprocal(out=wrec[:], in_=wsum[:]))
                op("vector", lambda e, R=R: e.tensor_scalar(
                    out=w8s[:, R * 8:(R + 1) * 8], in0=t8e[:], scalar1=wrec[:, 0:1],
                    scalar2=None, op0=ALU.mult))
                op("vector", lambda e: e.tensor_copy(out=d8f[:], in_=d8[:]))
                op("vector", lambda e, R=R: e.tensor_scalar(
                    out=g8f[:], in0=d8f[:], scalar1=rowg[:, 0:1],
                    scalar2=float(R * 128 * 8192), op0=ALU.add, op1=ALU.add))
                op("vector", lambda e, R=R: e.tensor_copy(
                    out=g8s[:, R * 8:(R + 1) * 8], in_=g8f[:]))
            vg8_all = P.n("V")

            _splits = {0: (6, 10), 1: (10, 12), 2: (12, 16), 3: (16, 18),
                       4: (18, 22), 5: (22, 24), 6: (24, 28), 7: (28, 32)}
            for R in range(8):
                pb = R % 2
                xch = "LXa" if pb == 0 else "LXb"
                xm2 = op("scalar", lambda e, R=R, pb=pb: e.dma_start(
                    out=x_t2[pb][:], in_=xin[R * 128:(R + 1) * 128, :]),
                    inc="LX2", k=16, auto=False,
                    waits=[("V", P.n("V")),
                           (xch, x2st_marks[R - 2] if R >= 2 else 0)])
                wait("gpsimd", ("V", vg8_all if R == 0 else P.n("V")),
                     ("LV", vv_marks[7]))
                gm = []
                mac_marks = []
                for k in range(8):
                    ach = "GA0" if k % 2 == 0 else "GA1"
                    gm.append(op("gpsimd", lambda e, R=R, k=k: e.indirect_dma_start(
                        out=gat[k % 2][:], out_offset=None, in_=vv_flat,
                        in_offset=IndirectOffsetOnAxis(
                            ap=g8s[:, R * 8 + k:R * 8 + k + 1], axis=1)),
                        inc=ach, k=16,
                        waits=[] if k < 2 else [("V", mac_marks[k - 2])]))
                    if k == 0:
                        op("vector", lambda e, R=R: e.tensor_scalar(
                            out=acc16[:], in0=gat[0][:],
                            scalar1=w8s[:, R * 8:R * 8 + 1],
                            scalar2=None, op0=ALU.mult),
                            waits=[(ach, gm[0])])
                        mac_marks.append(P.n("V"))
                    else:
                        op("vector", lambda e, R=R, k=k: e.scalar_tensor_tensor(
                            out=acc16[:], in0=gat[k % 2][:],
                            scalar=w8s[:, R * 8 + k:R * 8 + k + 1],
                            in1=acc16[:], op0=ALU.mult, op1=ALU.add),
                            waits=[(ach, gm[k])])
                        mac_marks.append(P.n("V"))
                op("vector", lambda e, pb=pb: e.tensor_tensor(
                    out=x_t2[pb][:], in0=x_t2[pb][:], in1=acc16[:], op=ALU.add),
                    waits=[("LX2", xm2)])
                x2st_marks[R] = op("sync", lambda e, R=R, pb=pb: e.dma_start(
                    out=x2dram[R * 128:(R + 1) * 128, :], in_=x_t2[pb][:]),
                    inc=xch, k=16, waits=[("V", P.n("V"))])
                for ch in range(8):
                    op("vector", lambda e, ch=ch, pb=pb: e.bn_stats(
                        out=stat[:, 6 * ch:6 * ch + 6],
                        in_=x_t2[pb][:, 512 * ch:512 * (ch + 1)]),
                        waits=[("A", last_sqrt[0])] if ch == 0 else ())
                op("vector", lambda e: e.bn_aggr(out=mv[:], in_=stat[:]))
                va2 = P.n("V")
                op("scalar", lambda e: e.activation(out=rstd_t[:], in_=mv[:, 1:2],
                   func=AF.Sqrt, bias=1e-5), waits=[("V", va2)])
                aa2 = P.n("A")
                last_sqrt[0] = aa2
                op("vector", lambda e: e.reciprocal(out=rstd[:], in_=rstd_t[:]),
                   waits=[("A", aa2)])
                op("vector", lambda e: e.tensor_scalar(
                    out=nmr[:], in0=mv[:, 0:1], scalar1=rstd[:, 0:1],
                    scalar2=-1.0, op0=ALU.mult, op1=ALU.mult))
                op("vector", lambda e, pb=pb: e.tensor_scalar(
                    out=y2bf[:], in0=x_t2[pb][:], scalar1=rstd[:, 0:1],
                    scalar2=nmr[:, 0:1], op0=ALU.mult, op1=ALU.add),
                    waits=[("T", max(last_y2tr_mark[0], last_tr_mark[0]))])
                y2r = y2bf[:].rearrange("p (q two) -> p two q", two=2)
                for c in range(32):
                    tw = op("tensor", lambda e, c=c: e.transpose(
                        p_tr[c % 2][:],
                        y2r[:, c // 16, (c % 16) * 128:(c % 16) * 128 + 128],
                        ident[:]), inc="T", waits=[("V", P.n("V"))])
                    op("vector", lambda e, c=c, R=R: e.tensor_copy(
                        out=xt[c][:, R * 128:(R + 1) * 128], in_=p_tr[c % 2][:]),
                        waits=[("T", tw)])
                last_y2tr_mark[0] = P.n("T")
                ln_marks[R] = P.n("V")
                emit_scatter_chunk(*_splits[R])

        v2b_total = P.n("V")
        a2b_total = P.n("A")
        lx2_total = P.n("LX2")
        lxa_total = P.n("LXa")
        lxb_total = P.n("LXb")
        with ExitStack() as fctx:
            Fa = fctx.enter_context
            wrd = [Fa(nc.sbuf_tensor(f"wrd{i}", [128, 512], BF16)) for i in range(2)]
            x2s = [Fa(nc.sbuf_tensor(f"x2s{i}", [128, 512], F32)) for i in range(2)]
            osb = [Fa(nc.sbuf_tensor(f"osb{i}", [128, 512], F32)) for i in range(2)]
            p_f = [Fa(nc.psum_tensor(f"pf{r}", [128, 512], F32)) for r in range(8)]

            ev_marks = [0] * 8
            st_marks = {}
            mmg_marks = {}
            for OB in range(8):
                for c in range(32):
                    gi = OB * 32 + c
                    pb = gi % 2
                    wch = "LWRa" if pb == 0 else "LWRb"
                    if c == 0:
                        wws = [("GPa", scat_mark_d[4 * OB + 2]),
                               ("GPb", scat_mark_d[4 * OB + 3]),
                               ("LW", lw_total)]
                        if OB == 0:
                            wws += [("V", v2b_total), ("LX2", lx2_total),
                                    ("LXa", lxa_total), ("LXb", lxb_total)]
                    else:
                        wws = [("T", mmg_marks.get(gi - 2, 0))]
                    wm = op("sync", lambda e, OB=OB, c=c, pb=pb: e.dma_start_transpose(
                        out=wrd[pb][:],
                        in_=wdram[OB * 512:(OB + 1) * 512, c * 128:(c + 1) * 128]),
                        inc=wch, k=16, waits=wws)
                    for r in range(8):
                        ws = [(wch, wm)] if r == 0 else []
                        if c == 0:
                            ws += [("V", ln_marks[r]), ("V", ev_marks[r]),
                                   ("A", a2b_total)]
                        op("tensor", lambda e, c=c, r=r, pb=pb: e.matmul(
                            p_f[r][:], xt[c][:, r * 128:(r + 1) * 128], wrd[pb][:],
                            start=(c == 0), stop=(c == 31)), inc="T", waits=ws)
                    mmg_marks[gi] = P.n("T")
                mmend = P.n("T")
                for r in range(8):
                    pb = r % 2
                    xch = "LXa" if pb == 0 else "LXb"
                    j = OB * 8 + r
                    sch = "STa" if j % 2 == 0 else "STb"
                    xws = [("V", v2b_total if (OB == 0 and r == 0) else P.n("V")),
                           (xch, x2st_marks[r])]
                    if OB == 0 and r == 0:
                        xws += [("LXa", lxa_total), ("LXb", lxb_total),
                                ("LX2", lx2_total)]
                    xm = op("scalar", lambda e, OB=OB, r=r, pb=pb: e.dma_start(
                        out=x2s[pb][:],
                        in_=x2dram[r * 128:(r + 1) * 128, OB * 512:(OB + 1) * 512]),
                        inc="LXS", k=16, auto=False, waits=xws)
                    op("vector", lambda e, r=r, pb=pb: e.tensor_tensor(
                        out=osb[pb][:], in0=x2s[pb][:], in1=p_f[r][:], op=ALU.add),
                        waits=[("LXS", xm), ("T", mmend),
                               (sch, st_marks.get(j - 2, 0))])
                    ev_marks[r] = P.n("V")
                    op("sync", lambda e, OB=OB, r=r, pb=pb: e.dma_start(
                        out=out_hbm[r * 128:(r + 1) * 128, OB * 512:(OB + 1) * 512],
                        in_=osb[pb][:]), inc=sch, k=16,
                        waits=[("V", ev_marks[r])])
                    st_marks[j] = P.n(sch)
            wait("sync", ("STa", P.n("STa")), ("STb", P.n("STb")))

        block = octx.enter_context(nc.Block())

        @block.sync
        def _(e):
            for f in P.q["sync"]:
                f(e)

        @block.scalar
        def _(e):
            for f in P.q["scalar"]:
                f(e)

        @block.vector
        def _(e):
            for f in P.q["vector"]:
                f(e)

        @block.gpsimd
        def _(e):
            with e.register("rb") as rb:
                e.reg_mov(rb, 4096 * 4096 - 1)
                P.breg = rb
                for f in P.q["gpsimd"]:
                    f(e)

        @block.tensor
        def _(e):
            for f in P.q["tensor"]:
                f(e)

    return nc


def host_inputs_for_core(core, X2d, bp, scales, vals, idx, ptr, consts):
    return dict(
        xin=np.ascontiguousarray(X2d[core * ROWS:(core + 1) * ROWS]),
        bpin=bp,
        sc_in=np.ascontiguousarray(scales.reshape(32, 128).T),
        ptr0_in=np.ascontiguousarray(ptr[:4096].reshape(32, 128).T),
        ptr1_in=np.ascontiguousarray(ptr[1:4097].reshape(32, 128).T),
        vals_in=np.ascontiguousarray(
            np.concatenate([vals, np.zeros(NNZ_PAD - len(vals), vals.dtype)]
                           ).reshape(-1, 256)),
        idxs_in=np.ascontiguousarray(
            np.concatenate([idx, np.zeros(NNZ_PAD - len(idx), idx.dtype)]
                           ).reshape(-1, 256)),
        **consts,
    )








from concourse.bass_utils import run_bass_kernel_spmd

_NC_CACHE = {}


def kernel(L0_attn_x, L0_ffn_base_packed, L0_ffn_scales, L0_ffn_ortho_vals,
           L0_ffn_ortho_indices, L0_ffn_ortho_ptr):
    X2d = np.ascontiguousarray(np.asarray(L0_attn_x, dtype=np.float32).reshape(-1, L))
    bp = np.ascontiguousarray(np.asarray(L0_ffn_base_packed, dtype=np.int32))
    scales = np.asarray(L0_ffn_scales, dtype=np.float32)
    vals = np.asarray(L0_ffn_ortho_vals, dtype=np.float32)
    idx = np.asarray(L0_ffn_ortho_indices, dtype=np.int32)
    ptr = np.asarray(L0_ffn_ortho_ptr, dtype=np.int32)
    consts = host_constants()
    pv, pi = host_csr_permute(vals, idx, ptr)
    if "nc" not in _NC_CACHE:
        _NC_CACHE["nc"] = build()
    nc = _NC_CACHE["nc"]
    in_maps = [host_inputs_for_core(c, X2d, bp, scales, pv, pi, ptr, consts)
               for c in range(8)]
    res = run_bass_kernel_spmd(nc, in_maps, core_ids=list(range(8)))
    out = np.concatenate([res.results[c]["out"] for c in range(8)], axis=0)
    return out.reshape(4, 2048, L).astype(np.float32)
